# version 02
# baseline (speedup 1.0000x reference)
import math
from contextlib import ExitStack

import numpy as np
import ml_dtypes

import concourse.bass as bass
import concourse.mybir as mybir
from concourse.bass_utils import run_bass_kernel_spmd

F32 = mybir.dt.float32
BF16 = mybir.dt.bfloat16
AF = mybir.ActivationFunctionType
ALU = mybir.AluOpType
AX = mybir.AxisListType
bfloat16 = ml_dtypes.bfloat16

B = 2
S = 4096
D = 4096
NH = 32
DN = 128
DR = 64
DV = 128
QL = 1536
KVL = 512
IDXH = 16
IDXD = 128
TOPK = 1024
EPS_LN = 1e-5
EPS_A = 1e-6
EPS_B = 1e-5
SCALE = 1.0 / math.sqrt(DN + DR)
IDX_SCALE = (IDXH ** -0.5) * (IDXD ** -0.5)
RANKS = (8, 16, 32, 64)
NCORE = 8
CT = 1024
NT = 8
NEG = -1.0e9
N_BISECT = 18
DEBUG_L2B = False


def ap(t, off, dims):
    return bass.AP(t, off, dims)



class Tok:
    __slots__ = ("eng", "idx")

    def __init__(self, eng, idx):
        self.eng = eng
        self.idx = idx


class Eng:
    def __init__(self, name, inc):
        self.name = name
        self.inc = inc
        self.ops = []

    def op(self, fn, deps=()):
        self.ops.append((fn, tuple(d for d in deps if d is not None)))
        return Tok(self, len(self.ops))


class Res:

    def __init__(self):
        self.w = None
        self.readers = []

    def wdeps(self):
        return tuple([self.w] + self.readers)

    def rdeps(self):
        return (self.w,)

    def wrote(self, tok):
        self.w = tok
        self.readers = []

    def read(self, tok):
        self.readers.append(tok)


class Sched:
    ENGS = ("sync", "gpsimd", "tensor", "vector", "scalar")

    def __init__(self, same_eng_waits=False):
        self.e = {n: Eng(n, 16 if n in ("sync", "gpsimd") else 1) for n in self.ENGS}
        self.same_eng_waits = same_eng_waits
        self.nwaits = 0

    def emit(self, nc, name="prog"):
        ctx = ExitStack()
        with ctx:
            sems = {n: ctx.enter_context(nc.semaphore("sem_" + n)) for n in self.ENGS}
            with nc.Block(name) as blk:
                for n in self.ENGS:
                    eng = self.e[n]
                    if not eng.ops:
                        continue

                    def body(e, eng=eng, n=n):
                        waited = {m: 0 for m in self.ENGS}
                        for (fn, deps) in eng.ops:
                            for d in deps:
                                if d.eng.name == n and not self.same_eng_waits:
                                    continue
                                v = d.idx * d.eng.inc
                                if waited[d.eng.name] < v:
                                    e.wait_ge(sems[d.eng.name], v)
                                    waited[d.eng.name] = v
                                    self.nwaits += 1
                            fn(e).then_inc(sems[n], eng.inc)

                    getattr(blk, n)(body)

    def dma(self, queue, out, in_, deps=()):
        return self.e[queue].op(lambda e, out=out, in_=in_: e.dma_start(out=out, in_=in_), deps)

    def mm(self, out, lhsT, rhs, start, stop, deps=(), transpose=False):
        def fn(e):
            return e.matmul(out=out, lhsT=lhsT, rhs=rhs, start=start, stop=stop,
                            is_transpose=(True if transpose else None),
                            skip_group_check=True)
        return self.e["tensor"].op(fn, deps)

    def act(self, out, in_, func, deps=(), bias=None, scale=None, accum_out=None):
        def fn(e):
            kw = {}
            if bias is not None:
                kw["bias"] = bias
            if scale is not None:
                kw["scale"] = scale
            if accum_out is not None:
                kw["accum_out"] = accum_out
            return e.activation(out=out, in_=in_, func=func, **kw)
        return self.e["scalar"].op(fn, deps)

    def v(self, method, *args, deps=(), **kw):
        return self.e["vector"].op(lambda e: getattr(e, method)(*args, **kw), deps)

    def g(self, method, *args, deps=(), **kw):
        return self.e["gpsimd"].op(lambda e: getattr(e, method)(*args, **kw), deps)



def _blk(wT, chunk=128):
    K, NO = wT.shape
    n = NO // chunk
    return np.ascontiguousarray(wT.reshape(K, n, chunk).transpose(1, 0, 2))


def _pe_planar_cols():
    cols = []
    for h in range(NH):
        base = h * 192
        cols.extend(range(base, base + DN))
    for h in range(NH):
        base = h * 192 + DN
        cols.extend(range(base, base + DR, 2))
        cols.extend(range(base + 1, base + DR, 2))
    return np.array(cols, dtype=np.int64)


def host_prep_shared(i):
    g = {}
    perm = _pe_planar_cols()
    g["a_wqa_blk"] = _blk(np.ascontiguousarray(i["L0_attn_a_wq_a"].T))
    g["a_wqb"] = np.ascontiguousarray(i["L0_attn_a_wq_b"].T[:, perm])
    wkv_aT = i["L0_attn_a_wkv_a"].T
    g["a_wkva_blk"] = _blk(np.ascontiguousarray(wkv_aT[:, :KVL]))
    g["a_wkva_pe"] = np.ascontiguousarray(wkv_aT[:, KVL:])
    g["a_wkvb"] = np.ascontiguousarray(i["L0_attn_a_wkv_b"].T)
    g["a_woT_blk"] = _blk(np.ascontiguousarray(i["L0_attn_a_wo"].T))
    g["a_qnw"] = np.ascontiguousarray(i["L0_attn_a_q_norm_w"].reshape(12, 128).T)
    g["a_kvnw"] = np.ascontiguousarray(i["L0_attn_a_kv_norm_w"].reshape(4, 128).T)
    g["b_wqa_blk"] = _blk(np.ascontiguousarray(i["L0_attn_b_q_a_w"]))
    g["b_wqb"] = np.ascontiguousarray(i["L0_attn_b_q_b_w"][:, perm])
    kvb_a = i["L0_attn_b_kv_a_w"]
    g["b_wkva_blk"] = _blk(np.ascontiguousarray(kvb_a[:, :KVL]))
    g["b_wkva_pe"] = np.ascontiguousarray(kvb_a[:, KVL:])
    g["b_wkvb"] = np.ascontiguousarray(i["L0_attn_b_kv_b_w"])
    g["b_oT_blk"] = _blk(np.ascontiguousarray(i["L0_attn_b_o_w"]))
    g["b_qnw"] = np.ascontiguousarray(i["L0_attn_b_q_a_norm_w"].reshape(12, 128).T)
    g["b_kvnw"] = np.ascontiguousarray(i["L0_attn_b_kv_a_norm_w"].reshape(4, 128).T)
    g["b_iwq"] = np.ascontiguousarray(i["L0_attn_b_idx_wq_w"])
    g["b_iwk"] = np.ascontiguousarray(i["L0_attn_b_idx_wk_w"])
    g["b_iww"] = np.ascontiguousarray(i["L0_attn_b_idx_weights_w"])
    g["f_WT_blk"] = _blk(np.ascontiguousarray(i["L0_ffn_W"].T))
    g["f_bias"] = np.ascontiguousarray(i["L0_ffn_b"].reshape(32, 128).T)
    g["f_AT"] = np.ascontiguousarray(i["L0_ffn_A"].T)
    g["f_BT_blk"] = _blk(np.ascontiguousarray(i["L0_ffn_B"].T))
    gT = np.ascontiguousarray(i["L0_ffn_gate_w"].T)
    g["f_gTr"] = np.ascontiguousarray(gT.reshape(32, 128, 4).transpose(1, 0, 2).reshape(128, 128))
    g["f_gbb"] = np.ascontiguousarray(np.broadcast_to(i["L0_ffn_gate_b"], (128, 4)).copy())
    patM = np.zeros((4, 64), dtype=bfloat16)
    for r, R in enumerate(RANKS):
        patM[r, :R] = 1
    g["f_patM"] = patM
    g["ident_bf"] = np.eye(128, dtype=bfloat16)
    g["epsv"] = np.broadcast_to(
        np.array([EPS_LN, EPS_A, EPS_B, 1.0], np.float32), (128, 4)).copy()
    return g


def host_prep_core(c):
    d = {}
    p = np.arange(128)
    j = np.arange(4)
    svals = (1024 * j[:, None] + 8 * p[None, :] + c).astype(np.float64)
    freqs = (1.0 / (10000.0 ** (np.arange(0, DR, 2, dtype=np.float32) / DR))).astype(np.float64)
    ang = svals[:, :, None] * freqs[None, None, :]
    d["cos_tm"] = np.ascontiguousarray(
        np.cos(ang).astype(np.float32).transpose(1, 0, 2).reshape(128, 4 * 32))
    d["sin_tm"] = np.ascontiguousarray(
        np.sin(ang).astype(np.float32).transpose(1, 0, 2).reshape(128, 4 * 32))
    kk = np.arange(1024)
    causal = kk[None, :] <= (8 * p[:, None] + c)
    d["c01"] = np.ascontiguousarray(causal.astype(bfloat16))
    d["cneg"] = np.ascontiguousarray(np.where(causal, 0.0, NEG).astype(bfloat16))
    d["zeros_vec"] = (1023.0 - 8.0 * p - c).reshape(128, 1).astype(np.float32)
    return d



class Bufs(dict):
    __getattr__ = dict.__getitem__

    def __setattr__(self, k, v):
        self[k] = v


def sb(ctx, nc, name, shape, dt):
    return ctx.enter_context(nc.sbuf_tensor("s_" + name, shape, dt))


def psb(ctx, nc, name, shape, dt):
    return ctx.enter_context(nc.psum_tensor("p_" + name, shape, dt))


def mkres(b, *names):
    for n in names:
        b["R_" + n] = Res()


def R(b, name) -> Res:
    return b["R_" + name]


def emit_ln(s, b, x_dram):
    last = None
    prev_tv1 = [None, None]
    for t in range(NT):
        bb, j = t // 4, t % 4
        xres = R(b, "xt%d" % (t % 2))
        xtv = b.xt[:, (t % 2) * 4096:(t % 2) * 4096 + 4096]
        c0 = (t % 2) * 8
        sl = lambda i: b.small[:, c0 + i:c0 + i + 1]
        tl = s.dma("sync", xtv, x_dram[bb, j * 128:(j + 1) * 128, :], deps=xres.wdeps())
        xres.wrote(tl)
        t1 = s.v("reduce_sum", deps=(tl,), out=sl(0), in_=xtv, axis=AX.X)
        xlr = R(b, "xln")
        t2 = s.act(b.xln[:], xtv, AF.Square,
                   deps=(tl, prev_tv1[t % 2]) + xlr.wdeps(),
                   accum_out=sl(1))
        xlr.wrote(t2)
        tm = s.v("tensor_scalar", deps=(t1,), out=sl(2), in0=sl(0),
                 scalar1=1.0 / D, scalar2=None, op0=ALU.mult)
        tv1 = s.v("tensor_scalar", deps=(t2,), out=sl(3), in0=sl(1),
                  scalar1=1.0 / D, scalar2=None, op0=ALU.mult)
        prev_tv1[t % 2] = tv1
        tm2 = s.v("tensor_tensor", deps=(tm,), out=sl(4), in0=sl(2),
                  in1=sl(2), op=ALU.mult)
        tva = s.v("tensor_tensor", deps=(tv1, tm2), out=sl(3), in0=sl(3),
                  in1=sl(4), op=ALU.subtract)
        tsd = s.act(sl(5), sl(3), AF.Sqrt, deps=(tva,), bias=b.epsv[:, 0:1])
        trs = s.v("reciprocal", deps=(tsd,), out=sl(6), in_=sl(5))
        tnm = s.v("tensor_scalar", deps=(trs, tm), out=sl(7), in0=sl(2),
                  scalar1=sl(6), scalar2=-1.0, op0=ALU.mult, op1=ALU.mult)
        tln = s.act(b.xln[:], xtv, AF.Identity, deps=(trs, tnm, t2) + xlr.wdeps(),
                    scale=sl(6), bias=sl(7))
        xlr.wrote(tln)
        xres.read(tln)
        pres = R(b, "ps_bf")
        for gidx in range(4):
            tt = None
            for ii in range(8):
                dd = gidx * 8 + ii
                tt = s.mm(b.ps_bf[:, ii * 128:(ii + 1) * 128],
                          b.xln[:, dd * 128:(dd + 1) * 128], b.ident[:],
                          start=(ii == 0), stop=(ii == 7),
                          deps=((tln,) + pres.wdeps()) if ii == 0 else (),
                          transpose=True)
            pres.wrote(tt)
            xlr.read(tt)
            dst = ap(b.xfm, (gidx * 8) * 1024 + t * 128,
                     [[32 * 1024, 128], [1024, 8], [1, 128]])
            ev = s.v("tensor_copy", deps=(tt,), out=dst, in_=b.ps_bf[:, 0:1024])
            pres.read(ev)
            last = ev
    return last


def emit_proj_rms(s, b, wblocks_dram, n_in, n_out, dst, nw, eps_col, xdep, slot):
    ev_toks = [[None, None] for _ in range(n_out)]
    rms_prev = [None, None]
    for o in range(n_out):
        par = o % 2
        wres = R(b, "wbf%d" % par)
        wdst = b.wbf[:, par * 4096:par * 4096 + n_in * 128]
        wt = s.dma("gpsimd", wdst,
                   ap(wblocks_dram.tensor, o * n_in * 128 * 128,
                      [[128, 128], [128 * 128, n_in], [1, 128]]),
                   deps=wres.wdeps())
        wres.wrote(wt)
        for ts in range(2):
            pres = R(b, "ps_a" if ts == 0 else "ps_b")
            psx = b.ps_a if ts == 0 else b.ps_b
            t = None
            for k in range(n_in):
                t = s.mm(psx[:, :],
                         b.wbf[:, par * 4096 + k * 128:par * 4096 + (k + 1) * 128],
                         b.xfm[:, k * 1024 + ts * 512:k * 1024 + ts * 512 + 512],
                         start=(k == 0), stop=(k == n_in - 1),
                         deps=((wt, xdep) + pres.wdeps()) if k == 0 else ())
            pres.wrote(t)
            wres.read(t)
            ev = s.act(dst[:, o * 1024 + ts * 512:o * 1024 + ts * 512 + 512],
                       psx[:, :], AF.Identity, deps=(t,))
            sqres = R(b, "sqf%d" % ts)
            sq = s.act(b.sqf[:, ts * 512:ts * 512 + 512], psx[:, :], AF.Square,
                       deps=(t,) + sqres.wdeps())
            sqres.wrote(sq)
            pres.read(ev)
            pres.read(sq)
            rt = s.mm(b.ps_row[:, slot * 1024 + ts * 512:slot * 1024 + ts * 512 + 512],
                      b.ones_c[:], b.sqf[:, ts * 512:ts * 512 + 512],
                      start=(o == 0), stop=(o == n_out - 1),
                      deps=(sq, rms_prev[ts]))
            sqres.read(rt)
            rms_prev[ts] = rt
            ev_toks[o][ts] = ev
    rr = b.rrow[:, slot * 1024:(slot + 1) * 1024]
    t1 = s.v("tensor_scalar", deps=tuple(rms_prev), out=rr,
             in0=b.ps_row[:, slot * 1024:(slot + 1) * 1024],
             scalar1=1.0 / (n_out * 128), scalar2=None, op0=ALU.mult)
    t2 = s.act(rr, rr, AF.Sqrt, deps=(t1,), bias=b.epsv[0:1, eps_col:eps_col + 1])
    t3 = s.v("reciprocal", deps=(t2,), out=rr, in_=rr)
    out_toks = []
    bres = R(b, "ps_bc")
    for ts in range(2):
        tb = s.mm(b.ps_bc[:, :], b.ones_r[:], rr[:, ts * 512:ts * 512 + 512],
                  start=True, stop=True, deps=(t3,) + bres.wdeps())
        bres.wrote(tb)
        for o in range(n_out):
            sl = slice(o * 1024 + ts * 512, o * 1024 + ts * 512 + 512)
            ot = s.v("scalar_tensor_tensor", deps=(tb, ev_toks[o][ts]), out=dst[:, sl],
                     in0=dst[:, sl], scalar=nw[:, o:o + 1], in1=b.ps_bc[:, :],
                     op0=ALU.mult, op1=ALU.mult)
            bres.read(ot)
            out_toks.append(ot)
    return out_toks[-1]


def emit_rhs_proj(s, b, wT_dram, K, NO, lhs, lhs_dep, out_cb):
    nk = K // 128
    wmax = min(512, (16 * 512) // nk)
    slices = []
    o = 0
    while o < NO:
        w = min(wmax, NO - o)
        slices.append((o, w))
        o += w
    for si, (off, w) in enumerate(slices):
        par = si % 2
        wres = R(b, "wrhsb%d" % par)
        wdst = b.wrhsb[:, par * (16 * 512):par * (16 * 512) + nk * w]
        wt = s.dma("gpsimd", wdst,
                   ap(wT_dram.tensor, off, [[NO, 128], [128 * NO, nk], [1, w]]),
                   deps=wres.wdeps())
        wres.wrote(wt)
        for t in range(NT):
            pres = R(b, "ps_a" if t % 2 == 0 else "ps_b")
            psx = b.ps_a if t % 2 == 0 else b.ps_b
            tmm = None
            for k in range(nk):
                tmm = s.mm(psx[:, :w],
                           lhs[:, k * 1024 + t * 128:k * 1024 + t * 128 + 128],
                           wdst[:, k * w:(k + 1) * w],
                           start=(k == 0), stop=(k == nk - 1),
                           deps=((wt, lhs_dep) + pres.wdeps()) if k == 0 else ())
            pres.wrote(tmm)
            wres.read(tmm)
            ct = out_cb(si, t, w, psx, tmm, off)
            pres.read(ct)


def store_cb(s, b, out_dram):
    def cb(si, t, w, psx, tmm, off):
        par = (si * NT + t) % 2
        eres = R(b, "ev%d" % par)
        evv = b.ev[:, par * 512:par * 512 + w]
        ev = s.act(evv, psx[:, :w], AF.Identity, deps=(tmm,) + eres.wdeps())
        eres.wrote(ev)
        st = s.dma("sync", out_dram[t * 128:(t + 1) * 128, off:off + w],
                   evv, deps=(ev,))
        eres.read(st)
        return st
    return cb



def build_L1(prefix):
    nc = bass.Bass("TRN2", target_bir_lowering=False)
    dr = lambda name, shape, dt, kind: nc.dram_tensor(name, shape, dt, kind=kind).ap()
    x0 = dr("x0", [B, 512, D], F32, "ExternalInput")
    wqa = dr("wqa_blk", [12, 4096, 128], F32, "ExternalInput")
    wqb = dr("wqb", [1536, 6144], F32, "ExternalInput")
    wkva = dr("wkva_blk", [4, 4096, 128], F32, "ExternalInput")
    wkvape = dr("wkva_pe", [4096, 64], F32, "ExternalInput")
    wkvb = dr("wkvb", [512, 8192], F32, "ExternalInput")
    qnw_d = dr("qnw", [128, 12], F32, "ExternalInput")
    kvnw_d = dr("kvnw", [128, 4], F32, "ExternalInput")
    ident_d = dr("ident_bf", [128, 128], BF16, "ExternalInput")
    epsv_d = dr("epsv", [128, 4], F32, "ExternalInput")
    cos_d = dr("cos_tm", [128, 128], F32, "ExternalInput")
    sin_d = dr("sin_tm", [128, 128], F32, "ExternalInput")
    q_tm = dr("q_tm", [1024, 6144], BF16, "ExternalOutput")
    kpe_tm = dr("kpe_tm", [1024, 64], BF16, "ExternalOutput")
    kvup_tm = dr("kvup_tm", [1024, 8192], BF16, "ExternalOutput")
    is_b = prefix == "b"
    if is_b:
        iwq_d = dr("iwq", [4096, 2048], F32, "ExternalInput")
        iwk_d = dr("iwk", [4096, 128], F32, "ExternalInput")
        iww_d = dr("iww", [4096, 16], F32, "ExternalInput")
        iq_tm = dr("iq_tm", [1024, 2048], BF16, "ExternalOutput")
        ik_tm = dr("ik_tm", [1024, 128], BF16, "ExternalOutput")
        iw_tm = dr("iw_tm", [1024, 16], BF16, "ExternalOutput")

    eps_col = 1 if prefix == "a" else 2
    ctx = ExitStack()
    with ctx:
        b = Bufs()
        b.ident = sb(ctx, nc, "ident", [128, 128], BF16)
        b.cos_tm = sb(ctx, nc, "cos_tm", [128, 128], F32)
        b.sin_tm = sb(ctx, nc, "sin_tm", [128, 128], F32)
        b.qnw = sb(ctx, nc, "qnw", [128, 12], F32)
        b.kvnw = sb(ctx, nc, "kvnw", [128, 4], F32)
        b.ones_c = sb(ctx, nc, "ones_c", [128, 1], F32)
        b.ones_r = sb(ctx, nc, "ones_r", [1, 128], F32)
        b.epsv = sb(ctx, nc, "epsv", [128, 4], F32)
        b.xt = sb(ctx, nc, "xt", [128, 2 * 4096], F32)
        b.xln = sb(ctx, nc, "xln", [128, 4096], BF16)
        b.xfm = sb(ctx, nc, "xfm", [128, 32 * 1024], BF16)
        b.small = sb(ctx, nc, "small", [128, 16], F32)
        b.qlat = sb(ctx, nc, "qlat", [128, 12 * 1024], BF16)
        b.kvlat = sb(ctx, nc, "kvlat", [128, 4 * 1024], BF16)
        b.sqf = sb(ctx, nc, "sqf", [128, 1024], F32)
        b.rsc = sb(ctx, nc, "rsc", [128, 1024], F32)
        b.rrow = sb(ctx, nc, "rrow", [1, 2 * 1024], F32)
        b.wbf = sb(ctx, nc, "wbf", [128, 2 * 4096], BF16)
        b.wrhsb = sb(ctx, nc, "wrhsb", [128, 2 * 16 * 512], BF16)
        b.ev = sb(ctx, nc, "ev", [128, 2 * 512], BF16)
        b.ps_a = psb(ctx, nc, "ps_a", [128, 512], F32)
        b.ps_b = psb(ctx, nc, "ps_b", [128, 512], F32)
        b.ps_bf = psb(ctx, nc, "ps_bf", [128, 1024], BF16)
        b.ps_row = psb(ctx, nc, "ps_row", [1, 2 * 1024], F32)
        b.ps_bc = psb(ctx, nc, "ps_bc", [128, 512], F32)
        mkres(b, "xt0", "xt1", "xln", "ps_bf", "ps_a", "ps_b", "ps_bc",
              "wbf0", "wbf1", "wrhsb0", "wrhsb1", "ev0", "ev1", "sqf0", "sqf1",
              "rsc")

        s = Sched(same_eng_waits=True)
        cdep = []
        for (buf, dd) in ((b.ident, ident_d), (b.cos_tm, cos_d), (b.sin_tm, sin_d),
                          (b.qnw, qnw_d), (b.kvnw, kvnw_d), (b.epsv, epsv_d)):
            cdep.append(s.dma("sync", buf[:], dd[:, :]))
        cdep.append(s.g("iota", out=b.ones_c[:], pattern=[[0, 1]], base=1,
                        channel_multiplier=0, allow_small_or_imprecise_dtypes=True))
        cdep.append(s.g("iota", out=b.ones_r[:], pattern=[[0, 128]], base=1,
                        channel_multiplier=0, allow_small_or_imprecise_dtypes=True))
        xdep = emit_ln(s, b, x0)
        xdep = s.v("tensor_copy", deps=(xdep,) + tuple(cdep), out=b.rsc[:, 0:1],
                   in_=b.small[:, 0:1])
        R(b, "rsc").wrote(xdep)

        qdep = emit_proj_rms(s, b, wqa, 32, 12, b.qlat, b.qnw, eps_col, xdep, slot=0)
        kvdep = emit_proj_rms(s, b, wkva, 32, 4, b.kvlat, b.kvnw, eps_col, xdep, slot=1)

        def kpe_cb(si, t, w, psx, tmm, off):
            j = t % 4
            cosv = b.cos_tm[:, j * 32:(j + 1) * 32]
            sinv = b.sin_tm[:, j * 32:(j + 1) * 32]
            xe = ap(psx, 0, [[512, 128], [2, 32]])
            xo = ap(psx, 1, [[512, 128], [2, 32]])
            par = t % 2
            eres = R(b, "ev%d" % par)
            re = ap(b.ev, par * 512, [[1024, 128], [2, 32]])
            ro = ap(b.ev, par * 512 + 1, [[1024, 128], [2, 32]])
            rres = R(b, "rsc")
            a1 = s.v("tensor_tensor", deps=(tmm,) + rres.wdeps(), out=b.rsc[:, 0:32], in0=xe, in1=cosv, op=ALU.mult)
            a2 = s.v("tensor_tensor", deps=(tmm,), out=b.rsc[:, 32:64], in0=xo, in1=sinv, op=ALU.mult)
            a3 = s.v("tensor_tensor", deps=(a1, a2) + eres.wdeps(), out=re, in0=b.rsc[:, 0:32],
                     in1=b.rsc[:, 32:64], op=ALU.subtract)
            a4 = s.v("tensor_tensor", deps=(tmm,), out=b.rsc[:, 64:96], in0=xo, in1=cosv, op=ALU.mult)
            a5 = s.v("tensor_tensor", deps=(tmm,), out=b.rsc[:, 96:128], in0=xe, in1=sinv, op=ALU.mult)
            a6 = s.v("tensor_tensor", deps=(a4, a5, a3), out=ro, in0=b.rsc[:, 64:96],
                     in1=b.rsc[:, 96:128], op=ALU.add)
            rres.wrote(a6)
            eres.wrote(a6)
            st = s.dma("sync", kpe_tm[t * 128:(t + 1) * 128, :], b.ev[:, par * 512:par * 512 + 64],
                       deps=(a6,))
            eres.read(st)
            return st
        emit_rhs_proj(s, b, wkvape, D, 64, b.xfm, xdep, kpe_cb)

        def q_cb(si, t, w, psx, tmm, off):
            if si < 8:
                return store_cb(s, b, q_tm)(si, t, w, psx, tmm, off)
            j = t % 4
            par = (si * NT + t) % 2
            eres = R(b, "ev%d" % par)
            rres = R(b, "rsc")
            cosv = ap(b.cos_tm, j * 32, [[128, 128], [0, 8], [1, 32]])
            sinv = ap(b.sin_tm, j * 32, [[128, 128], [0, 8], [1, 32]])
            xe = ap(psx, 0, [[512, 128], [64, 8], [1, 32]])
            xo = ap(psx, 32, [[512, 128], [64, 8], [1, 32]])
            re = ap(b.ev, par * 512, [[1024, 128], [64, 8], [1, 32]])
            ro = ap(b.ev, par * 512 + 32, [[1024, 128], [64, 8], [1, 32]])
            b1 = s.v("tensor_tensor", deps=(tmm,) + rres.wdeps(), out=b.rsc[:, 0:256], in0=xe, in1=cosv, op=ALU.mult)
            b2 = s.v("tensor_tensor", deps=(tmm,), out=b.rsc[:, 256:512], in0=xo, in1=sinv, op=ALU.mult)
            b3 = s.v("tensor_tensor", deps=(b1, b2) + eres.wdeps(), out=re, in0=b.rsc[:, 0:256],
                     in1=b.rsc[:, 256:512], op=ALU.subtract)
            b4 = s.v("tensor_tensor", deps=(tmm,), out=b.rsc[:, 512:768], in0=xo, in1=cosv, op=ALU.mult)
            b5 = s.v("tensor_tensor", deps=(tmm,), out=b.rsc[:, 768:1024], in0=xe, in1=sinv, op=ALU.mult)
            b6 = s.v("tensor_tensor", deps=(b4, b5, b3), out=ro, in0=b.rsc[:, 512:768],
                     in1=b.rsc[:, 768:1024], op=ALU.add)
            rres.wrote(b6)
            eres.wrote(b6)
            st = s.dma("sync", q_tm[t * 128:(t + 1) * 128, off:off + 512],
                       b.ev[:, par * 512:par * 512 + 512], deps=(b6,))
            eres.read(st)
            return st
        emit_rhs_proj(s, b, wqb, QL, 6144, b.qlat, qdep, q_cb)

        emit_rhs_proj(s, b, wkvb, KVL, 8192, b.kvlat, kvdep, store_cb(s, b, kvup_tm))

        if is_b:
            emit_rhs_proj(s, b, iwq_d, D, 2048, b.xfm, xdep, store_cb(s, b, iq_tm))
            emit_rhs_proj(s, b, iwk_d, D, 128, b.xfm, xdep, store_cb(s, b, ik_tm))

            def iw_cb(si, t, w, psx, tmm, off):
                par = t % 2
                eres = R(b, "ev%d" % par)
                evv = b.ev[:, par * 512:par * 512 + 16]
                ev = s.v("tensor_scalar", deps=(tmm,) + eres.wdeps(), out=evv,
                         in0=psx[:, :16], scalar1=IDX_SCALE, scalar2=None, op0=ALU.mult)
                eres.wrote(ev)
                st = s.dma("sync", iw_tm[t * 128:(t + 1) * 128, :], evv, deps=(ev,))
                eres.read(st)
                return st
            emit_rhs_proj(s, b, iww_d, D, 16, b.xfm, xdep, iw_cb)

        s.emit(nc, "L1" + prefix)
    return nc



def psdep(b, lo, hi, read=False):
    deps = ()
    for i in range(lo // 512, (hi + 511) // 512):
        r = R(b, "ps_s%d" % i)
        deps += r.rdeps() if read else r.wdeps()
    return deps


def psmark(b, lo, hi, tok, read=False):
    for i in range(lo // 512, (hi + 511) // 512):
        r = R(b, "ps_s%d" % i)
        (r.read if read else r.wrote)(tok)


def emit_flash(s, b, is_b, kn_fm, v_tm, kpe_fm, qn_fm, qp_fm):
    sdep = None
    for b_i in range(B):
        for h in range(NH):
            par = (b_i * NH + h) % 2
            qres = R(b, "q%d" % par)
            qwd = qres.wdeps()
            qn_t = s.dma("sync", b.qn[:, par * 512:par * 512 + 512],
                         qn_fm[b_i, h], deps=qwd)
            qp_t = s.dma("sync", b.qp[:, par * 512:par * 512 + 512],
                         qp_fm[b_i, h], deps=(qn_t,) + qwd)
            qres.wrote(qp_t)
            qn_sb = b.qn[:, par * 512:par * 512 + 512]
            qp_sb = b.qp[:, par * 512:par * 512 + 512]
            first = [True] * 4
            last_tu = [None] * 4
            last_tl = [None] * 4
            for ci, kc in enumerate((0, 2048)):
                kpar = (b_i * NH * 2 + h * 2 + ci) % 2
                kres = R(b, "kv%d" % kpar)
                kwd = kres.wdeps()
                kd = s.dma("sync", b.kb[:, kpar * 2048:kpar * 2048 + 2048],
                           kn_fm[b_i, h][:, kc:kc + 2048], deps=kwd)
                vd = s.dma("sync", b.vb[:, kpar * 2048:kpar * 2048 + 2048],
                           ap(v_tm.tensor, (b_i * NH + h) * 4096 * 128 + kc * 128,
                              [[128, 128], [128 * 128, 16], [1, 128]]), deps=(kd,) + kwd)
                kpbr = R(b, "kpbr")
                pd = s.dma("sync", b.kpb[:, 0:2048],
                           kpe_fm[b_i][:, kc:kc + 2048], deps=(vd,) + kwd + kpbr.wdeps())
                kres.wrote(pd)
                kpbr.wrote(pd)
                kbv = b.kb[:, kpar * 2048:]
                vbv = b.vb[:, kpar * 2048:]
                kpv = b.kpb
                for j in range(4):
                    w = min(2048, 1024 * (j + 1) - kc)
                    if w <= 0:
                        continue
                    tqk = None
                    for sl in range((w + 511) // 512):
                        sw = min(512, w - sl * 512)
                        t1 = s.mm(b.ps_s[:, sl * 512:sl * 512 + sw],
                                  qn_sb[:, j * 128:(j + 1) * 128],
                                  kbv[:, sl * 512:sl * 512 + sw],
                                  start=True, stop=False,
                                  deps=(pd, qp_t) + psdep(b, sl * 512, sl * 512 + sw))
                        tqk = s.mm(b.ps_s[:, sl * 512:sl * 512 + sw],
                                   qp_sb[:64, j * 128:(j + 1) * 128],
                                   kpv[:64, sl * 512:sl * 512 + sw],
                                   start=False, stop=True, deps=())
                        psmark(b, sl * 512, sl * 512 + sw, tqk)
                    R(b, "kpbr").read(tqk)
                    diag = 1024 * j - kc
                    has_diag = 0 <= diag < 2048
                    masked = is_b and j >= 1
                    if has_diag and not masked:
                        tqk = s.v("tensor_tensor", deps=(tqk,) + psdep(b, diag, diag + 1024),
                                  out=b.ps_s[:, diag:diag + 1024],
                                  in0=b.ps_s[:, diag:diag + 1024], in1=b.cneg[:],
                                  op=ALU.add)
                        psmark(b, diag, diag + 1024, tqk)
                    fj = b.fst[:, j * 8:j * 8 + 8]
                    mx = s.v("reduce_max", deps=(tqk,) + psdep(b, 0, w, read=False),
                             out=fj[:, 2:3], in_=b.ps_s[:, :w], axis=AX.X)
                    psmark(b, 0, w, mx, read=True)
                    if first[j]:
                        tm = s.v("tensor_copy", deps=(mx,), out=fj[:, 0:1], in_=fj[:, 2:3])
                        corr = None
                    else:
                        tnew = s.v("tensor_tensor", deps=(mx,), out=fj[:, 3:4],
                                   in0=fj[:, 0:1], in1=fj[:, 2:3], op=ALU.max)
                        tdiff = s.v("tensor_tensor", deps=(tnew,), out=fj[:, 4:5],
                                    in0=fj[:, 0:1], in1=fj[:, 3:4], op=ALU.subtract)
                        corr = s.act(fj[:, 5:6], fj[:, 4:5], AF.Exp, deps=(tdiff,),
                                     scale=SCALE)
                        tm = s.v("tensor_copy", deps=(corr,), out=fj[:, 0:1], in_=fj[:, 3:4])
                    nb = s.v("tensor_scalar", deps=(tm,), out=fj[:, 6:7], in0=fj[:, 0:1],
                             scalar1=-SCALE, scalar2=None, op0=ALU.mult)
                    pres = R(b, "pbuf")
                    accum = not (is_b and j >= 1)
                    te = s.act(b.pbuf[:, :w], b.ps_s[:, :w], AF.Exp,
                               deps=(nb,) + pres.wdeps() + psdep(b, 0, w, read=False),
                               scale=SCALE, bias=fj[:, 6:7],
                               accum_out=(fj[:, 7:8] if accum else None))
                    psmark(b, 0, w, te, read=True)
                    pres.wrote(te)
                    if is_b and j >= 1:
                        moff = b.moff[b_i][j] + kc
                        te = s.v("scalar_tensor_tensor", deps=(te,),
                                 out=b.pbuf[:, :w], in0=b.pbuf[:, :w], scalar=1.0,
                                 in1=b.masks[:, moff:moff + w], op0=ALU.mult,
                                 op1=ALU.mult, accum_out=fj[:, 7:8])
                        pres.wrote(te)
                    if first[j]:
                        tl = s.v("tensor_copy", deps=(te,), out=fj[:, 1:2], in_=fj[:, 7:8])
                    else:
                        tl = s.v("scalar_tensor_tensor", deps=(te, corr), out=fj[:, 1:2],
                                 in0=fj[:, 1:2], scalar=fj[:, 5:6], in1=fj[:, 7:8],
                                 op0=ALU.mult, op1=ALU.add)
                    tres = R(b, "ps_t")
                    tt = None
                    for kk in range(w // 128):
                        tt = s.mm(b.ps_t[:, kk * 128:(kk + 1) * 128],
                                  b.pbuf[:, kk * 128:(kk + 1) * 128], b.ident[:],
                                  start=(kk % 8 == 0), stop=(kk % 8 == 7 or kk == w // 128 - 1),
                                  deps=((te,) + tres.wdeps()) if kk == 0 else (),
                                  transpose=True)
                    tres.wrote(tt)
                    pres.read(tt)
                    ptres = R(b, "ptsb")
                    tev = s.v("tensor_copy", deps=(tt,) + ptres.wdeps(),
                              out=b.ptsb[:, :w], in_=b.ps_t[:, :w])
                    ptres.wrote(tev)
                    tres.read(tev)
                    ores = R(b, "ps_o")
                    tpv = None
                    for kk in range(w // 128):
                        tpv = s.mm(b.ps_o[:, 0:128],
                                   b.ptsb[:, kk * 128:(kk + 1) * 128],
                                   vbv[:, kk * 128:(kk + 1) * 128],
                                   start=(kk == 0), stop=(kk == w // 128 - 1),
                                   deps=((tev,) + ores.wdeps()) if kk == 0 else ())
                    ores.wrote(tpv)
                    ptres.read(tpv)
                    oa = b.oacc[:, j * 128:(j + 1) * 128]
                    if first[j]:
                        tu = s.act(oa, b.ps_o[:, 0:128], AF.Identity, deps=(tpv, tl))
                    else:
                        tu = s.v("scalar_tensor_tensor", deps=(tpv, tl), out=oa,
                                 in0=oa, scalar=fj[:, 5:6], in1=b.ps_o[:, 0:128],
                                 op0=ALU.mult, op1=ALU.add)
                    ores.read(tu)
                    kres.read(tu)
                    qres.read(tu)
                    last_tu[j] = tu
                    last_tl[j] = tl
                    first[j] = False
            for j in range(4):
                fj = b.fst[:, j * 8:j * 8 + 8]
                if is_b and DEBUG_L2B and h < 16:
                    s.v("tensor_copy", deps=(last_tl[j],),
                        out=b.dbg_l[:, (b_i * 16 + h) * 4 + j:(b_i * 16 + h) * 4 + j + 1],
                        in_=fj[:, 1:2])
                tli = s.v("reciprocal", deps=(last_tl[j], last_tu[j]), out=fj[:, 1:2], in_=fj[:, 1:2])
                pres = R(b, "pbuf")
                tob = s.v("tensor_scalar", deps=(tli,) + pres.wdeps(),
                          out=b.pbuf[:, j * 128:(j + 1) * 128],
                          in0=b.oacc[:, j * 128:(j + 1) * 128],
                          scalar1=fj[:, 1:2], scalar2=None, op0=ALU.mult)
                pres.wrote(tob)
                tres = R(b, "ps_t")
                ttr = s.mm(b.ps_t[:, 0:128], b.pbuf[:, j * 128:(j + 1) * 128], b.ident[:],
                           start=True, stop=True, deps=(tob,) + tres.wdeps(), transpose=True)
                tres.wrote(ttr)
                pres.read(ttr)
                tev = s.v("tensor_copy", deps=(ttr,),
                          out=b.big[:, h * 1024 + (b_i * 4 + j) * 128:
                                    h * 1024 + (b_i * 4 + j) * 128 + 128],
                          in_=b.ps_t[:, 0:128])
                tres.read(tev)
                sdep = tev
    return sdep


def emit_indexer_masks(s, b, iq_fm, ik_fm, iw_dep):
    mdep = None
    for b_i in range(B):
        ikr = R(b, "ikb")
        ikd = s.dma("sync", b.ikb[:], ik_fm[b_i], deps=ikr.wdeps())
        ikr.wrote(ikd)
        for j in range(4):
            kr = 1024 * (j + 1)
            ares = R(b, "acc")
            tacc = None
            for n in range(IDXH):
                npar = n % 2
                iqr = R(b, "iqb%d" % npar)
                iqd = s.dma("sync", b.iqb[:, npar * 512:npar * 512 + 512],
                            iq_fm[b_i, n], deps=iqr.wdeps())
                iqr.wrote(iqd)
                for sl in range(kr // 512):
                    ti = s.mm(b.ps_s[:, (sl % 4) * 512:(sl % 4) * 512 + 512],
                              b.iqb[:, npar * 512 + j * 128:npar * 512 + (j + 1) * 128],
                              b.ikb[:, sl * 512:(sl + 1) * 512],
                              start=True, stop=True,
                              deps=(iqd, ikd) + psdep(b, (sl % 4) * 512, (sl % 4) * 512 + 512))
                    psmark(b, (sl % 4) * 512, (sl % 4) * 512 + 512, ti)
                    rres = R(b, "rsc")
                    tr = s.act(b.rsc[:, 0:512], b.ps_s[:, (sl % 4) * 512:(sl % 4) * 512 + 512],
                               AF.Relu, deps=(ti,) + rres.wdeps())
                    rres.wrote(tr)
                    psmark(b, (sl % 4) * 512, (sl % 4) * 512 + 512, tr, read=True)
                    accsl = b.acc[:, sl * 512:(sl + 1) * 512]
                    iwcol = b.iwf[:, (b_i * 4 + j) * 16 + n:(b_i * 4 + j) * 16 + n + 1]
                    if n == 0:
                        tacc = s.v("tensor_scalar", deps=(tr,) + (ares.wdeps() if sl == 0 else ()),
                                   out=accsl, in0=b.rsc[:, 0:512],
                                   scalar1=iwcol, scalar2=None, op0=ALU.mult)
                    else:
                        tacc = s.v("scalar_tensor_tensor", deps=(tr, tacc), out=accsl,
                                   in0=b.rsc[:, 0:512], scalar=iwcol,
                                   in1=accsl, op0=ALU.mult, op1=ALU.add)
                    rres.read(tacc)
                    iqr.read(tacc)
                    ikr.read(tacc)
            ares.wrote(tacc)
            tz = s.v("tensor_tensor", deps=(tacc,), out=b.acc[:, kr - 1024:kr],
                     in0=b.acc[:, kr - 1024:kr], in1=b.c01[:], op=ALU.mult)
            ares.wrote(tz)
            if j == 0:
                continue
            w0 = b.wsp
            thi = s.v("reduce_max", deps=(tz,), out=w0[:, 0:1], in_=b.acc[:, :kr],
                      axis=AX.X, apply_absolute_value=True)
            tlo = s.v("tensor_scalar", deps=(thi,), out=w0[:, 1:2], in0=w0[:, 0:1],
                      scalar1=-1.0, scalar2=None, op0=ALU.mult)
            tprev = tlo
            for it in range(N_BISECT):
                t1 = s.v("tensor_tensor", deps=(tprev,), out=w0[:, 2:3], in0=w0[:, 0:1],
                         in1=w0[:, 1:2], op=ALU.add)
                t2 = s.v("tensor_scalar", deps=(t1,), out=w0[:, 2:3], in0=w0[:, 2:3],
                         scalar1=0.5, scalar2=None, op0=ALU.mult)
                ires = R(b, "ind")
                t3 = s.v("tensor_scalar", deps=(t2,) + ires.wdeps(), out=b.ind[:, :kr],
                         in0=b.acc[:, :kr], scalar1=w0[:, 2:3], scalar2=0.0,
                         op0=ALU.is_ge, op1=ALU.add, accum_out=w0[:, 3:4])
                ires.wrote(t3)
                t4 = s.v("tensor_scalar", deps=(t3,), out=w0[:, 4:5], in0=w0[:, 2:3],
                         scalar1=0.0, scalar2=None, op0=ALU.is_le)
                t5 = s.v("scalar_tensor_tensor", deps=(t4,), out=w0[:, 4:5],
                         in0=w0[:, 4:5], scalar=b.zeros_vec[:], in1=b.c1024[:],
                         op0=ALU.mult, op1=ALU.add)
                t6 = s.v("tensor_scalar", deps=(t5,), out=w0[:, 5:6], in0=w0[:, 3:4],
                         scalar1=w0[:, 4:5], scalar2=None, op0=ALU.is_ge)
                t7 = s.v("tensor_tensor", deps=(t6,), out=w0[:, 6:7], in0=w0[:, 2:3],
                         in1=w0[:, 1:2], op=ALU.subtract)
                t8 = s.v("tensor_tensor", deps=(t7,), out=w0[:, 6:7], in0=w0[:, 5:6],
                         in1=w0[:, 6:7], op=ALU.mult)
                t9 = s.v("tensor_tensor", deps=(t8,), out=w0[:, 1:2], in0=w0[:, 1:2],
                         in1=w0[:, 6:7], op=ALU.add)
                ta = s.v("tensor_tensor", deps=(t9,), out=w0[:, 6:7], in0=w0[:, 0:1],
                         in1=w0[:, 2:3], op=ALU.subtract)
                tb_ = s.v("tensor_tensor", deps=(ta,), out=w0[:, 6:7], in0=w0[:, 5:6],
                          in1=w0[:, 6:7], op=ALU.mult)
                tprev = s.v("tensor_tensor", deps=(tb_,), out=w0[:, 0:1], in0=w0[:, 2:3],
                            in1=w0[:, 6:7], op=ALU.add)
            moff = b.moff[b_i][j]
            tm1 = s.v("tensor_scalar", deps=(tprev,), out=b.masks[:, moff:moff + kr],
                      in0=b.acc[:, :kr], scalar1=w0[:, 1:2], scalar2=None, op0=ALU.is_ge)
            ares.read(tm1)
            mdep = s.v("tensor_tensor", deps=(tm1,), out=b.masks[:, moff + kr - 1024:moff + kr],
                       in0=b.masks[:, moff + kr - 1024:moff + kr], in1=b.c01[:], op=ALU.mult)
            if "dbg_cnt" in b:
                mdep = s.v("tensor_scalar", deps=(mdep,), out=b.ind[:, :kr],
                           in0=b.masks[:, moff:moff + kr], scalar1=0.5, scalar2=0.0,
                           op0=ALU.is_ge, op1=ALU.add,
                           accum_out=b.dbg_cnt[:, b_i * 4 + j:b_i * 4 + j + 1])
                R(b, "ind").wrote(mdep)
                for cc in range(7):
                    mdep = s.v("tensor_copy", deps=(mdep,),
                               out=b.dbg_w0[:, (b_i * 4 + j) * 7 + cc:(b_i * 4 + j) * 7 + cc + 1],
                               in_=w0[:, cc:cc + 1])
                mdep2 = s.dma("sync", b.dbg_acc_d[b_i * 4 + j][:, :kr],
                              b.acc[:, :kr], deps=(mdep,))
                ares.read(mdep2)
    return mdep


def emit_outproj_residual(s, b, wblk, src_fm, src_dep, xres, xout, n_out=32,
                          extra_cb=None, bias=None):
    last = None
    for o in range(n_out):
        par = o % 2
        wres = R(b, "wbf%d" % par)
        wdst = b.wbf[:, par * 4096:par * 4096 + 4096]
        wt = s.dma("gpsimd", wdst,
                   ap(wblk.tensor, o * 4096 * 128, [[128, 128], [128 * 128, 32], [1, 128]]),
                   deps=wres.wdeps())
        wres.wrote(wt)
        for ts in range(2):
            pres = R(b, "ps_o" if ts == 0 else "ps_x")
            psx = b.ps_o if ts == 0 else b.ps_x
            t = None
            for k in range(32):
                t = s.mm(psx[:, :], wdst[:, k * 128:(k + 1) * 128],
                         src_fm[:, k * 1024 + ts * 512:k * 1024 + ts * 512 + 512],
                         start=(k == 0), stop=(k == 31 and extra_cb is None),
                         deps=((wt, src_dep) + pres.wdeps()) if k == 0 else ())
            if extra_cb is not None:
                t = extra_cb(o, ts, psx)
            pres.wrote(t)
            wres.read(t)
            rres = R(b, "rsc")
            fv = b.rsc[:, ts * 512:ts * 512 + 512]
            tev = s.act(fv, psx[:, :], AF.Identity, deps=(t,) + rres.wdeps(),
                        bias=(bias[:, o:o + 1] if bias is not None else None))
            rres.wrote(tev)
            pres.read(tev)
            xrres = R(b, "xr4")
            txr = s.dma("sync", b.xr4[:],
                        ap(xres.tensor, ts * 512 * 4096 + o * 128,
                           [[4096, 128], [128 * 4096, 4], [1, 128]]),
                        deps=xrres.wdeps())
            xrres.wrote(txr)
            ttr = None
            for i in range(4):
                ttr = s.mm(b.ps_s[:, 1024 + i * 128:1024 + (i + 1) * 128],
                           fv[:, i * 128:(i + 1) * 128], b.identf[:],
                           start=(i == 0), stop=(i == 3),
                           deps=((tev,) + psdep(b, 1024, 1536)) if i == 0 else (),
                           transpose=True)
            psmark(b, 1024, 1536, ttr)
            rres.read(ttr)
            xsres = R(b, "xs4")
            tadd = s.v("tensor_tensor", deps=(ttr, txr) + xsres.wdeps(), out=b.xs4[:],
                       in0=b.xr4[:], in1=b.ps_s[:, 1024:1536], op=ALU.add)
            xsres.wrote(tadd)
            psmark(b, 1024, 1536, tadd, read=True)
            xrres.read(tadd)
            tst = s.dma("sync", ap(xout.tensor, ts * 512 * 4096 + o * 128,
                                   [[4096, 128], [128 * 4096, 4], [1, 128]]),
                        b.xs4[:], deps=(tadd,))
            xsres.read(tst)
            last = tst
    return last


def emit_ffn(s, b, WT_blk, AT_d, BT_blk, gT_d, gbb_d, patM_d, fbias_d, xin, xout):
    xdep = emit_ln(s, b, xin)
    gres = R(b, "gsb")
    tgw = s.dma("sync", b.gtr[:], gT_d[:, :])
    gcr = R(b, "ind") if "R_ind" in b else None
    tgc = s.v("tensor_copy", deps=(tgw,), out=b.gtb[:], in_=b.gtr[:])
    gb_toks = []
    for t in range(NT):
        tg = None
        for k in range(32):
            tg = s.mm(b.ps_s[:, 1536:1540], b.xfmbf_gate_lhs(k, t) if False else
                      b.xfm[:, k * 1024 + t * 128:k * 1024 + t * 128 + 128],
                      b.gtb[:, k * 4:(k + 1) * 4],
                      start=(k == 0), stop=(k == 31),
                      deps=((tgc, xdep) + psdep(b, 1536, 2048)) if k == 0 else ())
        psmark(b, 1536, 2048, tg)
        tga = s.v("tensor_tensor", deps=(tg,) + (gres.wdeps() if t == 0 else ()),
                  out=b.gsb[:, t * 4:t * 4 + 4], in0=b.ps_s[:, 1536:1540],
                  in1=b.gbb[:, :], op=ALU.add)
        psmark(b, 1536, 2048, tga, read=True)
        gb_toks.append(tga)
    gres.wrote(gb_toks[-1])
    sfm_dep = []
    for t in range(NT):
        tm4 = s.v("reduce_max", deps=(gb_toks[t],), out=b.wsp[:, 8:9],
                  in_=b.gsb[:, t * 4:t * 4 + 4], axis=AX.X)
        tsl = None
        for r in range(4):
            tsl = s.v("tensor_scalar", deps=(tm4,), out=b.selb[:, t * 4 + r:t * 4 + r + 1],
                      in0=b.gsb[:, t * 4 + r:t * 4 + r + 1], scalar1=b.wsp[:, 8:9],
                      scalar2=None, op0=ALU.is_ge)
        tres = R(b, "ps_t")
        ttr = s.mm(b.ps_t[:4, 0:128], b.selb[:, t * 4:t * 4 + 4], b.ident[:],
                   start=True, stop=True, deps=(tsl,) + tres.wdeps(), transpose=True)
        tres.wrote(ttr)
        tev = s.v("tensor_copy", deps=(ttr,), out=b.selfm[:4, t * 128:(t + 1) * 128],
                  in_=b.ps_t[:4, 0:128])
        tres.read(tev)
        sfm_dep.append(tev)
    patMr = R(b, "patm")
    tpm = s.dma("sync", b.patm[:4, :], patM_d[:, :], deps=patMr.wdeps())
    patMr.wrote(tpm)
    atr = R(b, "wbf0")
    tat = s.dma("gpsimd", b.wbf[:, 0:32 * 64],
                ap(AT_d.tensor, 0, [[64, 128], [128 * 64, 32], [1, 64]]),
                deps=atr.wdeps())
    atr.wrote(tat)
    uw_toks = []
    for ts in range(2):
        tu = None
        for k in range(32):
            tu = s.mm(b.ps_s[:64, 0:512], b.wbf[:, k * 64:(k + 1) * 64],
                      b.xfm[:, k * 1024 + ts * 512:k * 1024 + ts * 512 + 512],
                      start=(k == 0), stop=(k == 31),
                      deps=((tat, xdep) + psdep(b, 0, 512)) if k == 0 else ())
        psmark(b, 0, 512, tu)
        atr.read(tu)
        twf = s.mm(b.ps_s[:64, 512:1024], b.patm[:4, :],
                   b.selfm[:4, ts * 512:ts * 512 + 512],
                   start=True, stop=True, deps=(tpm,) + tuple(sfm_dep) + psdep(b, 512, 1024))
        psmark(b, 512, 1024, twf)
        patMr.read(twf)
        tue = s.act(b.ubf[:64, ts * 512:ts * 512 + 512], b.ps_s[:64, 0:512], AF.Identity,
                    deps=(tu,))
        psmark(b, 0, 512, tue, read=True)
        tuw = s.v("tensor_tensor", deps=(tue, twf), out=b.uwb[:64, ts * 512:ts * 512 + 512],
                  in0=b.ubf[:64, ts * 512:ts * 512 + 512], in1=b.ps_s[:64, 512:1024],
                  op=ALU.mult)
        psmark(b, 512, 1024, tuw, read=True)
        uw_toks.append(tuw)

    def lora_cb(o, ts, psx):
        tbm = s.mm(psx[:, :], b.btb[:64, o * 128:(o + 1) * 128],
                   b.uwb[:64, ts * 512:ts * 512 + 512],
                   start=False, stop=True, deps=(uw_toks[ts], R(b, "btb").w))
        return tbm
    btr = R(b, "btb")
    tbt = s.dma("gpsimd", b.btb[:64, :],
                ap(BT_blk.tensor, 0, [[128, 64], [64 * 128, 32], [1, 128]]),
                deps=btr.wdeps())
    btr.wrote(tbt)
    return emit_outproj_residual(s, b, WT_blk, b.xfm, xdep, xin, xout,
                                 extra_cb=lora_cb, bias=b.fbias)


def build_L2(prefix):
    nc = bass.Bass("TRN2", target_bir_lowering=False)
    drt = lambda name, shape, dt, kind: nc.dram_tensor(name, shape, dt, kind=kind)
    dr = lambda name, shape, dt, kind: drt(name, shape, dt, kind).ap()
    xres = dr("xres", [B, 512, D], F32, "ExternalInput")
    qn_fm = dr("qn_fm", [B, NH, 128, 512], BF16, "ExternalInput")
    qp_fm = dr("qp_fm", [B, NH, 64, 512], BF16, "ExternalInput")
    kn_fm = dr("kn_fm", [B, NH, 128, 4096], BF16, "ExternalInput")
    v_tm = dr("v_tm", [B, NH, 4096, 128], BF16, "ExternalInput")
    kpe_fm = dr("kpe_fm", [B, 64, 4096], BF16, "ExternalInput")
    wo_blk = dr("wo_blk", [32, 4096, 128], F32, "ExternalInput")
    WT_blk = dr("WT_blk", [32, 4096, 128], F32, "ExternalInput")
    fbias_d = dr("f_bias", [128, 32], F32, "ExternalInput")
    AT_d = dr("f_AT", [4096, 64], F32, "ExternalInput")
    BT_blk = dr("f_BT_blk", [32, 64, 128], F32, "ExternalInput")
    gT_d = dr("f_gTr", [128, 128], F32, "ExternalInput")
    gbb_d = dr("f_gbb", [128, 4], F32, "ExternalInput")
    patM_d = dr("f_patM", [4, 64], BF16, "ExternalInput")
    ident_d = dr("ident_bf", [128, 128], BF16, "ExternalInput")
    identf_d = dr("ident_f", [128, 128], F32, "ExternalInput")
    epsv_d = dr("epsv", [128, 4], F32, "ExternalInput")
    cneg_d = dr("cneg", [128, 1024], BF16, "ExternalInput")
    c01_d = dr("c01", [128, 1024], BF16, "ExternalInput")
    zeros_d = dr("zeros_vec", [128, 1], F32, "ExternalInput")
    c1024_d = dr("c1024", [128, 1], F32, "ExternalInput")
    xout_d = drt("xout", [B, 512, D], F32, "Internal")
    xfin = dr("xfin", [B, 512, D], F32, "ExternalOutput")
    is_b = prefix == "b"
    if is_b:
        iq_fm = dr("iq_fm", [B, IDXH, 128, 512], BF16, "ExternalInput")
        ik_fm = dr("ik_fm", [B, 128, 4096], BF16, "ExternalInput")
        iw_d = dr("iw_tm", [1024, 16], BF16, "ExternalInput")

    ctx = ExitStack()
    with ctx:
        b = Bufs()
        b.ident = sb(ctx, nc, "ident", [128, 128], BF16)
        b.identf = sb(ctx, nc, "identf", [128, 128], F32)
        b.epsv = sb(ctx, nc, "epsv", [128, 4], F32)
        b.cneg = sb(ctx, nc, "cneg", [128, 1024], BF16)
        b.c01 = sb(ctx, nc, "c01", [128, 1024], BF16)
        b.zeros_vec = sb(ctx, nc, "zeros_vec", [128, 1], F32)
        b.c1024 = sb(ctx, nc, "c1024", [128, 1], F32)
        b.qn = sb(ctx, nc, "qn", [128, 1024], BF16)
        b.qp = sb(ctx, nc, "qp", [64, 1024], BF16)
        b.kb = sb(ctx, nc, "kb", [128, 4096], BF16)
        b.vb = sb(ctx, nc, "vb", [128, 4096], BF16)
        b.kpb = sb(ctx, nc, "kpb", [64, 2048], BF16)
        b.pscr = sb(ctx, nc, "pscr", [128, 4096], BF16)
        b.pbuf = b.pscr[:, 0:2048]
        b.ptsb = b.pscr[:, 2048:4096]
        b.fst = sb(ctx, nc, "fst", [128, 32], F32)
        b.oacc = sb(ctx, nc, "oacc", [128, 512], F32)
        b.big = sb(ctx, nc, "big", [128, 32 * 1024], BF16)
        b.xfm = b.big
        b.xt = sb(ctx, nc, "xt", [128, 2 * 4096], F32)
        b.acc = b.xt
        b.xln = b.pscr
        b.small = sb(ctx, nc, "small", [128, 16], F32)
        b.wsp = sb(ctx, nc, "wsp", [128, 16], F32)
        b.rsc = sb(ctx, nc, "rsc", [128, 1024], F32)
        b.wbf = sb(ctx, nc, "wbf", [128, 2 * 4096], BF16)
        b.xr4 = sb(ctx, nc, "xr4", [128, 512], F32)
        b.xs4 = sb(ctx, nc, "xs4", [128, 512], F32)
        b.gsb = sb(ctx, nc, "gsb", [128, 32], F32)
        b.gtr = sb(ctx, nc, "gtr", [128, 128], F32)
        b.gtb = sb(ctx, nc, "gtb", [128, 128], BF16)
        b.gbb = sb(ctx, nc, "gbb", [128, 4], F32)
        b.selb = sb(ctx, nc, "selb", [128, 32], BF16)
        b.selfm = sb(ctx, nc, "selfm", [4, 1024], BF16)
        b.patm = sb(ctx, nc, "patm", [4, 64], BF16)
        b.ubf = sb(ctx, nc, "ubf", [64, 1024], BF16)
        b.uwb = sb(ctx, nc, "uwb", [64, 1024], BF16)
        b.fbias = sb(ctx, nc, "fbias", [128, 32], F32)
        if is_b:
            b.masks = sb(ctx, nc, "masks", [128, 2 * 9216], BF16)
            b.btb = b.masks[0:64, 0:4096]
            if DEBUG_L2B:
                b.dbg_cnt = sb(ctx, nc, "dbg_cnt", [128, 8], F32)
                b.dbg_w0 = sb(ctx, nc, "dbg_w0", [128, 64], F32)
            b.ind = b.pscr
            b.ikb = b.big[:, 0:4096]
            b.iqb = b.big[:, 4096:5120]
            b.iwf = sb(ctx, nc, "iwf", [128, 128], F32)
            b.moff = [{1: 0, 2: 2048, 3: 5120},
                      {1: 9216, 2: 11264, 3: 14336}]
        if not is_b:
            b.btb = sb(ctx, nc, "btb", [64, 32 * 128], BF16)
        b.ps_s = psb(ctx, nc, "ps_s", [128, 2048], F32)
        b.ps_t = psb(ctx, nc, "ps_t", [128, 2048], BF16)
        b.ps_o = psb(ctx, nc, "ps_o", [128, 512], F32)
        b.ps_x = psb(ctx, nc, "ps_x", [128, 512], F32)
        mkres(b, "xt0", "xt1", "xln", "ps_t", "ps_o", "ps_x", "ps_bc", "kpbr",
              "wbf0", "wbf1", "wrhsb0", "wrhsb1", "ev0", "ev1", "sqf0", "sqf1",
              "rsc", "q0", "q1", "kv0", "kv1", "pbuf", "ptsb", "acc", "ind",
              "ikb", "iqb0", "iqb1", "gsb", "xr4", "xs4", "patm", "btb",
              "ps_s0", "ps_s1", "ps_s2", "ps_s3")
        b.ps_bf = b.ps_t
        b["R_ps_bf"] = b["R_ps_t"]
        b.ps_a = b.ps_o
        b.ps_b = b.ps_x
        b["R_ps_a"] = b["R_ps_o"]
        b["R_ps_b"] = b["R_ps_x"]

        s = Sched(same_eng_waits=True)
        cdep = []
        for (buf, dd) in ((b.ident, ident_d), (b.identf, identf_d), (b.epsv, epsv_d),
                          (b.cneg, cneg_d), (b.c01, c01_d), (b.zeros_vec, zeros_d),
                          (b.c1024, c1024_d), (b.gbb, gbb_d), (b.fbias, fbias_d)):
            cdep.append(s.dma("sync", buf[:], dd[:, :]))
        if is_b and DEBUG_L2B:
            b.dbg_l = sb(ctx, nc, "dbg_l", [128, 512], F32)
            b.dbg_cnt_d = dr("dbg_cnt", [128, 8], F32, "ExternalOutput")
            b.dbg_masks_d = dr("dbg_masks", [128, 18432], BF16, "ExternalOutput")
            b.dbg_l_d = dr("dbg_l", [128, 8 * 64], F32, "ExternalOutput")
            b.dbg_w0_d = dr("dbg_w0", [128, 64], F32, "ExternalOutput")
            b.dbg_acc_d = dr("dbg_acc", [8, 128, 4096], F32, "ExternalOutput")
        if is_b:
            tiw = s.dma("sync", b.ind[:, 0:128],
                        ap(iw_d.tensor, 0, [[16, 128], [128 * 16, 8], [1, 16]]))
            tiwc = s.v("tensor_copy", deps=(tiw,), out=b.iwf[:], in_=b.ind[:, 0:128])
            cdep.append(tiwc)
            mdep = emit_indexer_masks(s, b, iq_fm, ik_fm, tiwc)
            cdep.append(mdep)
        seed = s.v("tensor_copy", deps=tuple(cdep), out=b.wsp[:, 15:16], in_=b.wsp[:, 15:16])
        R(b, "q0").wrote(seed)
        R(b, "q1").wrote(seed)
        R(b, "kv0").wrote(seed)
        R(b, "kv1").wrote(seed)
        R(b, "wbf0").wrote(seed)
        R(b, "wbf1").wrote(seed)

        adep = emit_flash(s, b, is_b, kn_fm, v_tm, kpe_fm, qn_fm, qp_fm)
        if is_b and DEBUG_L2B:
            td1 = s.dma("sync", b.dbg_cnt_d[:, :], b.dbg_cnt[:], deps=(adep,))
            td2 = s.dma("sync", b.dbg_masks_d[:, :], b.masks[:], deps=(adep,))
            td3 = s.dma("sync", b.dbg_l_d[:, :], b.dbg_l[:], deps=(adep,))
            td4 = s.dma("sync", b.dbg_w0_d[:, :], b.dbg_w0[:], deps=(adep,))
        opdep = emit_outproj_residual(s, b, wo_blk, b.big, adep, xres, xout_d.ap())
        R(b, "xt0").wrote(opdep)
        R(b, "xt1").wrote(opdep)
        R(b, "xln").wrote(opdep)
        R(b, "pbuf").wrote(opdep)
        R(b, "ptsb").wrote(opdep)
        R(b, "btb").wrote(opdep)
        fdep = emit_ffn(s, b, WT_blk, AT_d, BT_blk, gT_d, gbb_d, patM_d, fbias_d,
                        xout_d.ap(), xfin)
        s.emit(nc, "L2" + prefix)
    return nc



def _gather_kv(kvup_list, kpe_list):
    A = np.stack(kvup_list, axis=0).reshape(NCORE, B, 4, 128, 8192)
    A = np.ascontiguousarray(A.transpose(1, 2, 3, 0, 4)).reshape(B, S, NH, 256)
    kn_fm = np.ascontiguousarray(A[..., :128].transpose(0, 2, 3, 1))
    v_tm = np.ascontiguousarray(A[..., 128:].transpose(0, 2, 1, 3))
    P = np.stack(kpe_list, axis=0).reshape(NCORE, B, 4, 128, 64)
    P = np.ascontiguousarray(P.transpose(1, 2, 3, 0, 4)).reshape(B, S, 64)
    Ppl = np.concatenate([P[..., 0::2], P[..., 1::2]], axis=-1)
    kpe_fm = np.ascontiguousarray(Ppl.transpose(0, 2, 1))
    return kn_fm, v_tm, kpe_fm


def _split_q(q_tm):
    qr = q_tm.reshape(B, 512, 6144)
    qn = np.ascontiguousarray(qr[:, :, :4096].reshape(B, 512, NH, 128).transpose(0, 2, 3, 1))
    qp = np.ascontiguousarray(qr[:, :, 4096:].reshape(B, 512, NH, 64).transpose(0, 2, 3, 1))
    return qn, qp


def _l1_maps(prefix, g, dcs, xs_list):
    ms = []
    for c in range(NCORE):
        m = {"x0": xs_list[c],
             "wqa_blk": g[prefix + "_wqa_blk"], "wqb": g[prefix + "_wqb"],
             "wkva_blk": g[prefix + "_wkva_blk"], "wkva_pe": g[prefix + "_wkva_pe"],
             "wkvb": g[prefix + "_wkvb"], "qnw": g[prefix + "_qnw"],
             "kvnw": g[prefix + "_kvnw"], "ident_bf": g["ident_bf"],
             "epsv": g["epsv"], "cos_tm": dcs[c]["cos_tm"], "sin_tm": dcs[c]["sin_tm"]}
        if prefix == "b":
            m["iwq"] = g["b_iwq"]
            m["iwk"] = g["b_iwk"]
            m["iww"] = g["b_iww"]
        ms.append(m)
    return ms


def _l2_maps(prefix, g, dcs, xs_list, r1, identf, c1024):
    kvs = _gather_kv([r["kvup_tm"] for r in r1], [r["kpe_tm"] for r in r1])
    kn_fm, v_tm, kpe_fm = kvs
    wo = g["a_woT_blk"] if prefix == "a" else g["b_oT_blk"]
    ms = []
    if prefix == "b":
        ik_all = np.stack([r["ik_tm"] for r in r1], axis=0).reshape(NCORE, B, 4, 128, 128)
        ik_all = np.ascontiguousarray(ik_all.transpose(1, 2, 3, 0, 4)).reshape(B, S, 128)
        ik_fm = np.ascontiguousarray(ik_all.transpose(0, 2, 1))
    for c in range(NCORE):
        qn, qp = _split_q(r1[c]["q_tm"])
        m = {"xres": xs_list[c], "qn_fm": qn, "qp_fm": qp, "kn_fm": kn_fm,
             "v_tm": v_tm, "kpe_fm": kpe_fm, "wo_blk": wo,
             "WT_blk": g["f_WT_blk"], "f_bias": g["f_bias"], "f_AT": g["f_AT"],
             "f_BT_blk": g["f_BT_blk"], "f_gTr": g["f_gTr"], "f_gbb": g["f_gbb"],
             "f_patM": g["f_patM"], "ident_bf": g["ident_bf"], "ident_f": identf,
             "epsv": g["epsv"], "cneg": dcs[c]["cneg"], "c01": dcs[c]["c01"],
             "zeros_vec": dcs[c]["zeros_vec"], "c1024": c1024}
        if prefix == "b":
            iq = r1[c]["iq_tm"].reshape(B, 512, IDXH, 128)
            m["iq_fm"] = np.ascontiguousarray(iq.transpose(0, 2, 3, 1))
            m["ik_fm"] = ik_fm
            m["iw_tm"] = r1[c]["iw_tm"]
        ms.append(m)
    return ms


def kernel(**inputs):
    g = host_prep_shared(inputs)
    dcs = [host_prep_core(c) for c in range(NCORE)]
    identf = np.eye(128, dtype=np.float32)
    c1024 = np.full((128, 1), 1024.0, np.float32)
    x = inputs["L0_attn_a_x"]
    xs_list = [np.ascontiguousarray(x[:, c::8, :]) for c in range(NCORE)]
    cores = list(range(NCORE))

    nc = build_L1("a")
    r1 = run_bass_kernel_spmd(nc, _l1_maps("a", g, dcs, xs_list), core_ids=cores).results
    nc = build_L2("a")
    r2 = run_bass_kernel_spmd(nc, _l2_maps("a", g, dcs, xs_list, r1, identf, c1024),
                              core_ids=cores).results
    x2_list = [np.ascontiguousarray(r["xfin"]) for r in r2]
    nc = build_L1("b")
    r3 = run_bass_kernel_spmd(nc, _l1_maps("b", g, dcs, x2_list), core_ids=cores).results
    nc = build_L2("b")
    r4 = run_bass_kernel_spmd(nc, _l2_maps("b", g, dcs, x2_list, r3, identf, c1024),
                              core_ids=cores).results

    out = np.empty((B, S, D), np.float32)
    for c in range(NCORE):
        out[:, c::8, :] = r4[c]["xfin"]
    return out
